# version 05
# baseline (speedup 1.0000x reference)
import numpy as np
from contextlib import ExitStack

import concourse.bass as bass
import concourse.mybir as mybir
from concourse.bass_utils import run_bass_kernel_spmd

F32 = mybir.dt.float32
BF16 = mybir.dt.bfloat16
AF = mybir.ActivationFunctionType
AL = mybir.AluOpType

HID = 4096
T = 1024
TH = T + 3
NP = 8
NK = 32
NCH = 8
HPP = 16
NPK = 8

ENGINES = ("sync", "tensor", "vector", "scalar", "gpsimd")


class Sem:
    def __init__(self, h):
        self.h = h
        self.n = 0


def build_program():
    nc = bass.Bass("TRN2", target_bir_lowering=False)

    def din(name, shape, dt=F32):
        return nc.dram_tensor(name, shape, dt, kind="ExternalInput").ap()

    x_fm = din("x_fm", [HID, TH])
    wt = din("wt", [NP, 18, 128, HID])
    wdt = din("wdt", [128, HID])
    convw = din("convw", [128, 320])
    convb = din("convb", [128, 80])
    dtbias = din("dtbias", [128, 1])
    alog = din("alog", [128, 1])
    ddiag = din("ddiag", [NP, 128, 1024])
    owt = din("owt", [32, 128, 8192])
    normw = din("normw", [128, 64])
    fwt = din("fwt", [32, 128, HID])
    lat = din("lat", [HID, 128])
    lbm = din("lbm", [128, HID])
    lwcol = din("lwcol", [128, 1])
    fbias = din("fbias", [128, 32])
    addmask = din("addmask", [128, 128])
    allones = din("allones", [128, 128])
    eyef = din("eyef", [128, 128])
    eyeb = din("eyeb", [128, 128], BF16)
    cmask = din("cmask", [128, 1])
    consts = din("consts", [128, 2])

    y_out = nc.dram_tensor("y_out", [T, HID], F32, kind="ExternalOutput").ap()

    def dscr(name, shape, dt=F32):
        return nc.dram_tensor(name, shape, dt).ap()

    sg_hbm = dscr("sg_hbm", [8192, T], BF16)
    gn_hbm = dscr("gn_hbm", [8192, T], BF16)
    u1_hbm = dscr("u1_hbm", [HID, T])
    u_hbm = dscr("u_hbm", [HID, T])
    u2_hbm = dscr("u2_hbm", [T, HID])
    cc_in = [dscr("cc_in%d" % g, [128, 1024]) for g in range(NP)]
    cc_out = [dscr("cc_out%d" % g, [256, 1024]) for g in range(NP)]

    st = ExitStack()

    def sbuf(name, shape, dt):
        return st.enter_context(nc.sbuf_tensor(name, shape, dt))

    def psum(name, shape, dt=F32):
        return st.enter_context(nc.psum_tensor(name, shape, dt))

    bfa_t = sbuf("bfa", [128, 71680], BF16)
    f32_t = sbuf("f32a", [128, 17184], F32)

    class Arena:
        def __init__(self, t, n):
            self.t = t; self.n = n; self.off = 0
        def take(self, c):
            o = self.off; self.off += c
            assert self.off <= self.n, (self.off, self.n)
            return self.t[:, o:o + c]
        def mark(self):
            return self.off
        def reset(self, o):
            self.off = o

    BA = Arena(bfa_t, 71680)
    FA = Arena(f32_t, 17184)

    eyeb_sb = BA.take(128)
    onesb_sb = BA.take(128)
    eyef_sb = FA.take(128)
    onesf_sb = FA.take(128)
    mask_sb = FA.take(128)
    small_sb = FA.take(8)
    convw_sb = FA.take(320)
    convb_sb = FA.take(80)
    normw_sb = FA.take(64)
    fbias_sb = FA.take(32)

    markB = BA.mark(); markF = FA.mark()

    x_bf = BA.take(32 * TH)
    slab = BA.take(10 * TH)
    xT = BA.take(NCH * 1024)
    slabS = BA.take(NCH * 1024)
    cexp_sb = BA.take(2048)
    stq_sb = BA.take(2048)
    sg_sb = BA.take(1024)
    g_sb = BA.take(1024)
    gn_sb = BA.take(1024)
    gate_sb = BA.take(1024)
    ddiag_sb = BA.take(1024)
    bw_sb = BA.take(2 * 128)
    sq_sb = BA.take(2 * 128)
    sdp_sb = BA.take(2 * 64)
    wq_b = BA.take(2 * 1024)

    cum_sb = FA.take(1024)
    cumT_sb = FA.take(1024)
    dtT_sb = FA.take(1024)
    wT_sb = FA.take(1024)
    cdec_p = FA.take(128)
    dp_p = FA.take(128)
    srunA = FA.take(1024)
    srunB = FA.take(1024)
    seff_sb = FA.take(1024)
    e4_sb = FA.take(512)
    dm_sb = FA.take(512)
    ebc_sb = FA.take(512)
    xin_sb = FA.take(TH)
    wq_f = FA.take(2 * 1024)
    sin_sb = wq_f[:, 1024:2048]
    acc_sb = FA.take(2 * 1024)
    stage_c = acc_sb[0:1, 0:2048]
    stage_p = seff_sb[0:1, 0:256]
    cl_sb = FA.take(32)
    dlw_sb = FA.take(256)

    pA = psum("pA", [128, 512])
    pB = psum("pB", [128, 512])
    pC = psum("pC", [128, 512])
    pD = psum("pD", [128, 512])
    pE = psum("pE", [128, 512])
    pF = psum("pF", [128, 512])
    pT = psum("pT", [128, 1024], BF16)
    pU = psum("pU", [128, 512])

    sems = {}
    for s in ("dS", "dG", "tS", "vS", "sS", "gS", "cS", "dO"):
        sems[s] = Sem(st.enter_context(nc.semaphore(s)))

    seq = {e: [] for e in ENGINES}

    def E(e, fn):
        seq[e].append(fn)

    def W(e, s, tgt=None):
        sem = sems[s]
        if tgt is None:
            tgt = sem.n
        assert tgt <= sem.n, (s, tgt, sem.n)
        if tgt > 0:
            E(e, lambda eng, h=sem.h, t=tgt: eng.wait_ge(h, t))

    import os
    SERIAL = bool(int(os.environ.get("BASS_SERIAL_DEBUG", "1")))
    SCOPES = set(filter(None, os.environ.get("BASS_SERIAL_SCOPES", "").split(",")))
    ENG_SEM = {"tensor": "tS", "vector": "vS", "scalar": "sS", "gpsimd": "gS",
               "sync": None}
    cur_scope = ["pre"]

    def scope(name):
        cur_scope[0] = name

    def I(e, build, s=None, amt=1):
        if SERIAL or cur_scope[0] in SCOPES:
            for ss in ("tS", "vS", "sS", "gS", "dS", "dG", "dO"):
                W(e, ss)
            if s is None and ENG_SEM[e] is not None:
                s = ENG_SEM[e]
                amt = 1
        if s is None:
            E(e, lambda eng, b=build: b(eng))
            return None
        sem = sems[s]
        sem.n += amt
        E(e, lambda eng, b=build, h=sem.h, a=amt: b(eng).then_inc(h, a))
        return sem.n

    def DMA(e, out, in_, s="dS"):
        return I(e, lambda eng, o=out, i=in_: eng.dma_start(out=o, in_=i), s, 16)

    def barrier(engine):
        for s in ("tS", "vS", "sS", "gS", "dS", "dG"):
            W(engine, s)

    DMA("sync", eyeb_sb, eyeb)
    DMA("sync", eyef_sb, eyef)
    DMA("sync", mask_sb, addmask)
    DMA("sync", onesf_sb, allones)
    DMA("sync", convw_sb, convw)
    DMA("sync", convb_sb, convb)
    DMA("sync", normw_sb, normw)
    DMA("sync", fbias_sb, fbias)
    DMA("sync", small_sb[:, 0:1], dtbias)
    DMA("sync", small_sb[:, 1:2], alog)
    DMA("sync", small_sb[:, 3:4], cmask)
    DMA("sync", small_sb[:, 4:5], lwcol)
    tcon = DMA("sync", small_sb[:, 5:7], consts)
    W("vector", "dS", tcon)
    I("vector", lambda v: v.tensor_copy(onesb_sb, onesf_sb), "vS")
    W("scalar", "dS", tcon)
    I("scalar", lambda s: s.activation(small_sb[:, 2:3], small_sb[:, 1:2], AF.Exp), "sS")
    I("scalar", lambda s: s.mul(small_sb[:, 2:3], small_sb[:, 2:3], -1.0), "sS")
    t_small = I("scalar", lambda s: s.mul(small_sb[:, 4:5], small_sb[:, 4:5], 2.0), "sS")

    x_cast = [0] * 32
    for k in range(32):
        if k >= 1:
            W("sync", "sS", x_cast[k - 1])
        td = DMA("sync", xin_sb, x_fm[k * 128:(k + 1) * 128, :])
        W("scalar", "dS", td)
        x_cast[k] = I("scalar", lambda s, k=k: s.activation(
            x_bf[:, k * TH:(k + 1) * TH], xin_sb, AF.Copy), "sS")
    t_xcast = x_cast[-1]

    wq_use = [0, 0]
    wuse = wq_use
    for k in range(NK):
        buf = k % 2
        W("sync", "tS", wuse[buf])
        td = DMA("sync", wq_f[:, buf * 1024:buf * 1024 + 128], wdt[:, k * 128:(k + 1) * 128])
        W("scalar", "dS", td)
        tc = I("scalar", lambda s, buf=buf: s.activation(
            wq_b[:, buf * 1024:buf * 1024 + 128], wq_f[:, buf * 1024:buf * 1024 + 128],
            AF.Copy), "sS")
        W("tensor", "sS", tc)
        stat = wq_b[:, buf * 1024:buf * 1024 + 128]
        xk = x_bf[:, k * TH:(k + 1) * TH]
        a, b = (k == 0), (k == NK - 1)
        I("tensor", lambda t, s=stat, m=xk, a=a, b=b: t.matmul(
            pA[:, :], s, m[:, 3:515], start=a, stop=b))
        wuse[buf] = I("tensor", lambda t, s=stat, m=xk, a=a, b=b: t.matmul(
            pB[:, :], s, m[:, 515:1027], start=a, stop=b), "tS")
    t_dtmm = wuse[(NK - 1) % 2]
    dt_sb = srunA
    dA_sb = srunB
    W("scalar", "tS", t_dtmm)
    t_dt = 0
    for fc, bank in ((0, pA), (1, pB)):
        W("scalar", "vS")
        I("scalar", lambda s, bank=bank: s.activation(
            e4_sb, bank[:, :], AF.Abs, bias=small_sb[:, 0:1]))
        I("scalar", lambda s: s.activation(e4_sb, e4_sb, AF.Exp, scale=-1.0))
        I("scalar", lambda s: s.activation(e4_sb, e4_sb, AF.Ln, bias=small_sb[:, 6:7]))
        tr_ = I("scalar", lambda s, bank=bank: s.activation(
            dm_sb, bank[:, :], AF.Relu, bias=small_sb[:, 0:1]), "sS")
        W("vector", "sS", tr_)
        t_dt = I("vector", lambda v, fc=fc: v.tensor_tensor(
            dt_sb[:, fc * 512:(fc + 1) * 512], e4_sb, dm_sb, AL.add), "vS")
    W("vector", "sS", t_dt) if False else None
    W("vector", "vS", t_dt)
    I("vector", lambda v: v.tensor_scalar(dA_sb, dt_sb, small_sb[:, 2:3], None,
                                          AL.mult, AL.bypass), "vS")
    t_cum = 0
    for c in range(NCH):
        t_cum = I("vector", lambda v, c=c: v.tensor_tensor_scan(
            cum_sb[:, c * 128:(c + 1) * 128], dA_sb[:, c * 128:(c + 1) * 128],
            dA_sb[:, c * 128:(c + 1) * 128], 0.0, AL.add, AL.bypass), "vS")
    W("tensor", "vS", t_cum)
    for (src, dst) in ((cum_sb, cumT_sb), (dt_sb, dtT_sb)):
        for c in range(NCH):
            reg = pU[:, 0:128] if c % 2 == 0 else pU[:, 128:256]
            W("tensor", "vS")
            tt = I("tensor", lambda t, s=src, c=c, r=reg: t.transpose(
                r, s[:, c * 128:(c + 1) * 128], eyef_sb), "tS")
            W("vector", "tS", tt)
            I("vector", lambda v, d=dst, c=c, r=reg: v.tensor_copy(
                d[:, c * 128:(c + 1) * 128], r), "vS")
    for c in range(NCH):
        W("scalar", "vS")
        tl = I("scalar", lambda s, c=c: s.activation(
            dlw_sb[:, 0:128], cum_sb[:, c * 128:(c + 1) * 128], AF.Exp,
            bias=cum_sb[:, c * 128 + 127:c * 128 + 128], scale=-1.0), "sS")
        W("vector", "sS", tl)
        tw = I("vector", lambda v, c=c: v.tensor_tensor(
            dlw_sb[:, 128:256], dt_sb[:, c * 128:(c + 1) * 128], dlw_sb[:, 0:128],
            AL.mult), "vS")
        W("tensor", "vS", tw)
        tt = I("tensor", lambda t: t.transpose(pU[:, 256:384], dlw_sb[:, 128:256],
                                               eyef_sb), "tS")
        W("vector", "tS", tt)
        I("vector", lambda v, c=c: v.tensor_copy(wT_sb[:, c * 128:(c + 1) * 128],
                                                 pU[:, 256:384]), "vS")
    tg = I("vector", lambda v: v.tensor_copy(cl_sb[:, 0:8], cum_sb[:, 127:1024:128]),
           "vS")
    I("vector", lambda v: v.tensor_copy(cl_sb[:, 8:16], cl_sb[:, 0:8]), "vS")
    a_off, b_off = 8, 24
    for s in (1, 2, 4):
        I("vector", lambda v, a=a_off, b=b_off, s=s: v.tensor_tensor(
            cl_sb[:, b + s:b + 8], cl_sb[:, a + s:a + 8], cl_sb[:, a:a + 8 - s],
            AL.add), "vS")
        I("vector", lambda v, a=a_off, b=b_off, s=s: v.tensor_copy(
            cl_sb[:, b:b + s], cl_sb[:, a:a + s]), "vS")
        a_off, b_off = b_off, a_off
    I("vector", lambda v: v.memset(cl_sb[:, 16:17], 0.0), "vS")
    t_cl = I("vector", lambda v, a=a_off: v.tensor_copy(
        cl_sb[:, 17:24], cl_sb[:, a:a + 7]), "vS")

    last_ddiag = 0
    acc_free = [0, 0]
    gate_dma = 0
    gn_dma = 0
    last_gate_tick = 0

    for g in range(NP):
        barrier("tensor")

        scope("ptop")
        W("sync", "sS", last_ddiag)
        W("sync", "vS")
        td = DMA("sync", xin_sb[:, 0:1024], ddiag[g, :, :])
        W("scalar", "dS", td)
        last_ddiag = I("scalar", lambda s: s.activation(ddiag_sb, xin_sb[:, 0:1024],
                                                        AF.Copy), "sS")
        W("sync", "tS")
        tdp1 = DMA("sync", stage_p[0:1, 0:128], cl_sb[g * HPP:(g + 1) * HPP, 0:8])
        tdp2 = DMA("sync", stage_p[0:1, 128:256], cl_sb[g * HPP:(g + 1) * HPP, 16:24])
        W("tensor", "dS", tdp2)
        W("tensor", "sS")
        I("tensor", lambda t: t.matmul(pU[:, 0:128], onesf_sb[0:1, 0:128],
                                       stage_p[0:1, 0:128], start=True, stop=True))
        tbp = I("tensor", lambda t: t.matmul(pU[:, 128:256], onesf_sb[0:1, 0:128],
                                             stage_p[0:1, 128:256],
                                             start=True, stop=True), "tS")
        W("scalar", "tS", tbp)
        I("scalar", lambda s: s.activation(cdec_p, pU[:, 0:128], AF.Exp), "sS")
        t_cdp = I("scalar", lambda s: s.activation(dp_p, pU[:, 128:256], AF.Exp), "sS")
        W("tensor", "sS", t_cdp)

        scope("A1")
        evac_tick = {}
        for m in range(18):
            ev = m % 2
            b0, b1 = (pA, pB) if ev == 0 else (pD, pE)
            bh = pC if ev == 0 else pF
            conv_m = m < 10
            if m >= 2:
                es, et = evac_tick[m - 2]
                W("tensor", es, et)
            for ko in range(4):
                buf = ko % 2
                W("sync", "tS", wq_use[buf])
                td = DMA("sync", wq_f[:, buf * 1024:(buf + 1) * 1024],
                         wt[g, m, :, ko * 1024:(ko + 1) * 1024])
                W("scalar", "dS", td)
                tc = I("scalar", lambda s, buf=buf: s.activation(
                    wq_b[:, buf * 1024:(buf + 1) * 1024],
                    wq_f[:, buf * 1024:(buf + 1) * 1024], AF.Copy), "sS")
                W("tensor", "sS", tc)
                if g == 0 and m == 0 and ko == 0:
                    W("tensor", "sS", t_xcast)
                for kl in range(8):
                    k = ko * 8 + kl
                    stat = wq_b[:, buf * 1024 + kl * 128:buf * 1024 + (kl + 1) * 128]
                    xk = x_bf[:, k * TH:(k + 1) * TH]
                    a, bb_ = (k == 0), (k == NK - 1)
                    I("tensor", lambda t, s=stat, m_=xk, a=a, b=bb_, b0=b0: t.matmul(
                        b0[:, :], s, m_[:, 3:515], start=a, stop=b))
                    tl = I("tensor", lambda t, s=stat, m_=xk, a=a, b=bb_, b1=b1: t.matmul(
                        b1[:, :], s, m_[:, 515:1027], start=a, stop=b), "tS")
                    if conv_m:
                        tl = I("tensor", lambda t, s=stat, m_=xk, a=a, b=bb_, bh=bh: t.matmul(
                            bh[:, 0:3], s, m_[:, 0:3], start=a, stop=b), "tS")
                wq_use[buf] = sems["tS"].n
            t_mm = sems["tS"].n
            if conv_m:
                W("vector", "tS", t_mm)
                I("vector", lambda v, m=m, b0=b0: v.tensor_copy(
                    slab[:, m * TH + 3:m * TH + 515], b0[:, :]))
                I("vector", lambda v, m=m, b1=b1: v.tensor_copy(
                    slab[:, m * TH + 515:m * TH + 1027], b1[:, :]))
                tev = I("vector", lambda v, m=m, bh=bh: v.tensor_copy(
                    slab[:, m * TH:m * TH + 3], bh[:, 0:3]), "vS")
                evac_tick[m] = ("vS", tev)
            else:
                gi = m - 10
                W("scalar", "tS", t_mm)
                W("scalar", "dG", gate_dma)
                I("scalar", lambda s, b0=b0: s.activation(gate_sb[:, 0:512], b0[:, :],
                                                          AF.Silu))
                te = I("scalar", lambda s, b1=b1: s.activation(gate_sb[:, 512:1024],
                                                               b1[:, :], AF.Silu), "sS")
                evac_tick[m] = ("sS", te)
                W("gpsimd", "sS", te)
                gate_dma = DMA("gpsimd",
                               sg_hbm[g * 1024 + gi * 128:g * 1024 + (gi + 1) * 128, :],
                               gate_sb, s="dG")

        scope("conv")
        t_conv = 0
        for ti in range(10):
            buf = ti % 2
            acc = acc_sb[:, buf * 1024:(buf + 1) * 1024]
            es, et = evac_tick[ti]
            W("vector", es, et)
            W("vector", "sS", acc_free[buf])
            wc = lambda tap: convw_sb[:, (g * 10 + ti) * 4 + tap:(g * 10 + ti) * 4 + tap + 1]
            sl = lambda tap: slab[:, ti * TH + tap:ti * TH + tap + 1024]
            I("vector", lambda v, a=acc, w=wc(0), s_=sl(0): v.tensor_scalar(
                a, s_, w, None, AL.mult, AL.bypass))
            for tap in range(1, 4):
                tgp = I("vector", lambda v, a=acc, w=wc(tap), s_=sl(tap):
                        v.scalar_tensor_tensor(a, s_, w, a, AL.mult, AL.add), "vS")
            W("scalar", "vS", tgp)
            t_conv_v = tgp
            t_conv = I("scalar", lambda s, a=acc, ti=ti, g=g: s.activation(
                slab[:, ti * TH + 3:ti * TH + 1027], a, AF.Silu,
                bias=convb_sb[:, g * 10 + ti:g * 10 + ti + 1]), "sS")
            acc_free[buf] = t_conv

        scope("loopA")
        W("tensor", evac_tick[17][0], evac_tick[17][1])
        scan_done = [0] * NCH
        for c in range(NCH):
            W("tensor", "sS", t_conv)
            tt = 0
            for ti in range(8):
                tt = I("tensor", lambda t, ti=ti, c=c: t.transpose(
                    pT[:, ti * 128:(ti + 1) * 128],
                    slab[:, ti * TH + 3 + c * 128:ti * TH + 3 + (c + 1) * 128],
                    eyeb_sb), "tS")
            W("vector", "tS", tt)
            txT = I("vector", lambda v, c=c: v.tensor_copy(
                xT[:, c * 1024:(c + 1) * 1024], pT[:, :]), "vS")
            W("tensor", "vS", txT)
            tt = I("tensor", lambda t, c=c: t.transpose(
                pT[:, 0:128], slab[:, 8 * TH + 3 + c * 128:8 * TH + 3 + (c + 1) * 128],
                eyeb_sb), "tS")
            W("vector", "tS", tt)
            tBT = I("vector", lambda v: v.tensor_copy(dlw_sb[:, 0:128], pT[:, 0:128]),
                    "vS")
            sbank = (pE, pF)
            tm_h = [0] * HPP
            for h in range(HPP):
                habs = g * HPP + h
                buf = h % 2
                if h >= 2:
                    W("vector", "tS", tm_h[h - 2])
                tw = I("vector", lambda v, c=c, habs=habs, buf=buf: v.tensor_scalar(
                    bw_sb[:, buf * 128:(buf + 1) * 128], dlw_sb[:, 0:128],
                    wT_sb[:, c * 128 + habs:c * 128 + habs + 1], None,
                    AL.mult, AL.bypass), "vS")
                W("tensor", "vS", tw)
                bank = sbank[0] if h < 8 else sbank[1]
                off = (h % 8) * 64
                tm_h[h] = I("tensor", lambda t, c=c, h=h, bank=bank, off=off, buf=buf:
                            t.matmul(bank[:, off:off + 64],
                                     bw_sb[:, buf * 128:(buf + 1) * 128],
                                     xT[:, c * 1024 + h * 64:c * 1024 + (h + 1) * 64],
                                     start=True, stop=True), "tS")
            t_states = tm_h[-1]
            W("vector", "tS", t_states)
            if c == 0:
                I("vector", lambda v: v.tensor_copy(srunA[:, 0:512], pE[:, :]))
                tv = I("vector", lambda v: v.tensor_copy(srunA[:, 512:1024], pF[:, :]),
                       "vS")
                cur = srunA
            else:
                src = srunA if c % 2 == 1 else srunB
                dst = srunB if c % 2 == 1 else srunA
                for h in range(HPP):
                    bank = pE if h < 8 else pF
                    off = (h % 8) * 64
                    tv = I("vector", lambda v, h=h, c=c, src=src, dst=dst,
                           bank=bank, off=off: v.scalar_tensor_tensor(
                               dst[:, h * 64:(h + 1) * 64], src[:, h * 64:(h + 1) * 64],
                               cdec_p[:, h * 8 + c:h * 8 + c + 1],
                               bank[:, off:off + 64], AL.mult, AL.add), "vS")
                cur = dst
            scan_done[c] = I("vector", lambda v, c=c, cur=cur: v.tensor_copy(
                slabS[:, c * 1024:(c + 1) * 1024], cur), "vS")
        s_final = srunB if (NCH - 1) % 2 == 1 else srunA

        scope("ag")
        W("gpsimd", "vS", scan_done[NCH - 1])
        td = DMA("gpsimd", cc_in[g][:, :], s_final, s="dG")
        W("gpsimd", "dG", td)
        tcc = I("gpsimd", lambda gp, g=g: gp.collective_compute(
            "AllGather", AL.bypass,
            replica_groups=[[0, 1], [2, 3], [4, 5], [6, 7]],
            ins=[cc_in[g].opt()], outs=[cc_out[g].opt()]), "cS")
        W("gpsimd", "cS", tcc)
        W("gpsimd", "sS")
        td = DMA("gpsimd", sin_sb, cc_out[g][0:128, :], s="dG")
        W("vector", "dG", td)
        t_seff = I("vector", lambda v: v.tensor_scalar(
            seff_sb, sin_sb, small_sb[:, 3:4], None, AL.mult, AL.bypass), "vS")

        scope("loopB")
        e4_tick = [0] * 4
        ebc_tick = [0] * 4
        dm_tick = [0] * 4
        last_bc = 0
        for c in range(NCH):
            scope("lb_bc")
            W("sync", "dG", gate_dma)
            W("sync", "vS", last_gate_tick)
            for sub in range(8):
                tsg = DMA("sync", sg_sb[:, sub * 128:(sub + 1) * 128],
                          sg_hbm[g * 1024 + sub * 128:g * 1024 + (sub + 1) * 128,
                                 c * 128:(c + 1) * 128])
            rb = pC if c % 2 == 0 else pD
            tr = I("tensor", lambda t, c=c, rb=rb: t.matmul(
                rb[:, 0:128], slab[:, 8 * TH + 3 + c * 128:8 * TH + 3 + (c + 1) * 128],
                slab[:, 9 * TH + 3 + c * 128:9 * TH + 3 + (c + 1) * 128],
                start=True, stop=True), "tS")
            W("sync", "tS", last_bc)
            if c == 0:
                W("sync", "vS", t_conv_v)
                W("sync", "sS", t_conv)
            tstg = DMA("sync", stage_c,
                       cum_sb[g * HPP:(g + 1) * HPP, c * 128:(c + 1) * 128])
            st_done = [0] * HPP
            cexp_done = [0] * 4
            for q in range(4):
                bb = pA if q % 2 == 0 else pB
                W("tensor", "sS", ebc_tick[(q + 2) % 4])
                W("tensor", "vS", dm_tick[(q + 2) % 4])
                if q == 0:
                    W("tensor", "dS", tstg)
                tbc = 0
                for hl in range(4):
                    h = q * 4 + hl
                    tbc = I("tensor", lambda t, hl=hl, h=h, bb=bb: t.matmul(
                        bb[:, hl * 128:(hl + 1) * 128], onesf_sb[0:1, 0:128],
                        stage_c[0:1, h * 128:(h + 1) * 128],
                        start=True, stop=True), "tS")
                last_bc = tbc
                if q >= 1:
                    W("vector", "sS", e4_tick[q - 1])
                W("vector", "tS", tbc)
                for hl in range(4):
                    h = q * 4 + hl
                    habs = g * HPP + h
                    dm_tick[q] = I("vector", lambda v, hl=hl, habs=habs, c=c, bb=bb:
                                   v.scalar_tensor_tensor(
                                       dm_sb[:, hl * 128:(hl + 1) * 128],
                                       bb[:, hl * 128:(hl + 1) * 128],
                                       cumT_sb[:, c * 128 + habs:c * 128 + habs + 1],
                                       mask_sb, AL.subtract, AL.add), "vS")
                W("scalar", "vS", dm_tick[q])
                e4_tick[q] = I("scalar", lambda s: s.activation(e4_sb, dm_sb, AF.Exp),
                               "sS")
                ebc_tick[q] = I("scalar", lambda s, bb=bb: s.activation(
                    ebc_sb, bb[:, :], AF.Exp), "sS")
                W("vector", "sS", ebc_tick[q])
                cexp_done[q] = I("vector", lambda v, q=q, c=c: v.tensor_tensor(
                    cexp_sb[:, q * 512:(q + 1) * 512],
                    slab[:, 9 * TH + 3 + c * 128:9 * TH + 3 + (c + 1) * 128]
                    .rearrange("p (o f) -> p o f", o=1).broadcast_to([128, 4, 128]),
                    ebc_sb.rearrange("p (a b) -> p a b", a=4), AL.mult), "vS")
                W("vector", "tS", tr)
                for hl in range(4):
                    h = q * 4 + hl
                    habs = g * HPP + h
                    st_done[h] = I("vector", lambda v, hl=hl, h=h, habs=habs, c=c,
                                   rb=rb: v.scalar_tensor_tensor(
                                       stq_sb[:, h * 128:(h + 1) * 128],
                                       e4_sb[:, hl * 128:(hl + 1) * 128],
                                       dtT_sb[:, c * 128 + habs:c * 128 + habs + 1],
                                       rb[:, 0:128], AL.mult, AL.mult), "vS")
            scope("lb_y")
            ty = 0
            trm = 0
            for pk in range(NPK):
                yb = pE if pk % 2 == 0 else pF
                hA, hB = 2 * pk, 2 * pk + 1
                W("tensor", "sS", last_ddiag)
                W("tensor", "vS", max(st_done[hA], st_done[hB],
                                      cexp_done[hA // 4], cexp_done[hB // 4], t_seff))
                I("tensor", lambda t, pk=pk, c=c, yb=yb: t.matmul(
                    yb[:, 0:128], ddiag_sb[:, pk * 128:(pk + 1) * 128],
                    slab[:, pk * TH + 3 + c * 128:pk * TH + 3 + (c + 1) * 128],
                    start=True, stop=False))
                for hi, h in enumerate((hA, hB)):
                    habs = g * HPP + h
                    tp = (0, 64 * hi)
                    out = yb[0:64, 0:128] if hi == 0 else yb[64:128, 0:128]
                    I("tensor", lambda t, h=h, c=c, out=out, tp=tp: t.matmul(
                        out, xT[:, c * 1024 + h * 64:c * 1024 + (h + 1) * 64],
                        stq_sb[:, h * 128:(h + 1) * 128],
                        start=False, stop=False, tile_position=tp))
                    if c > 0:
                        I("tensor", lambda t, h=h, c=c, out=out, tp=tp: t.matmul(
                            out, slabS[:, (c - 1) * 1024 + h * 64:(c - 1) * 1024 + (h + 1) * 64],
                            cexp_sb[:, h * 128:(h + 1) * 128],
                            start=False, stop=False, tile_position=tp))
                    buf = h % 2
                    tsd = I("vector", lambda v, h=h, c=c, buf=buf:
                            v.tensor_scalar(
                                sdp_sb[:, buf * 64:(buf + 1) * 64],
                                seff_sb[:, h * 64:(h + 1) * 64],
                                dp_p[:, h * 8 + c:h * 8 + c + 1], None,
                                AL.mult, AL.bypass), "vS")
                    W("tensor", "vS", tsd)
                    ty = I("tensor", lambda t, h=h, out=out, tp=tp, buf=buf,
                           last=(hi == 1): t.matmul(
                               out, sdp_sb[:, buf * 64:(buf + 1) * 64],
                               cexp_sb[:, h * 128:(h + 1) * 128],
                               start=False, stop=last, tile_position=tp), "tS")
                W("vector", "tS", ty)
                if pk == 0:
                    W("vector", "dS", tsg)
                I("vector", lambda v, pk=pk, yb=yb: v.tensor_tensor(
                    g_sb[:, pk * 128:(pk + 1) * 128], yb[:, 0:128],
                    sg_sb[:, pk * 128:(pk + 1) * 128], AL.mult), "vS")
                buf = pk % 2
                tsq = I("vector", lambda v, pk=pk, buf=buf: v.tensor_tensor(
                    sq_sb[:, buf * 128:(buf + 1) * 128],
                    g_sb[:, pk * 128:(pk + 1) * 128],
                    g_sb[:, pk * 128:(pk + 1) * 128], AL.mult), "vS")
                W("tensor", "vS", tsq)
                trm = I("tensor", lambda t, pk=pk, buf=buf: t.matmul(
                    pU[:, 0:128], onesb_sb, sq_sb[:, buf * 128:(buf + 1) * 128],
                    start=(pk == 0), stop=(pk == NPK - 1)), "tS")
            scope("lb_rms")
            W("scalar", "tS", trm)
            I("scalar", lambda s: s.activation(dlw_sb[:, 128:256], pU[:, 0:128], AF.Ln,
                                               bias=small_sb[:, 5:6], scale=1.0 / 1024))
            trs = I("scalar", lambda s: s.activation(dlw_sb[:, 128:256],
                                                     dlw_sb[:, 128:256], AF.Exp,
                                                     scale=-0.5), "sS")
            W("vector", "sS", trs)
            W("vector", "dG", gn_dma)
            tgn = 0
            for sti in range(8):
                tgn = I("vector", lambda v, sti=sti, g=g: v.scalar_tensor_tensor(
                    gn_sb[:, sti * 128:(sti + 1) * 128],
                    g_sb[:, sti * 128:(sti + 1) * 128],
                    normw_sb[:, g * 8 + sti:g * 8 + sti + 1], dlw_sb[:, 128:256],
                    AL.mult, AL.mult), "vS")
            last_gate_tick = tgn
            W("gpsimd", "vS", tgn)
            for sub in range(8):
                gn_dma = DMA("gpsimd",
                             gn_hbm[g * 1024 + sub * 128:g * 1024 + (sub + 1) * 128,
                                    c * 128:(c + 1) * 128],
                             gn_sb[:, sub * 128:(sub + 1) * 128], s="dG")

    scope("B")
    BA.reset(markB)
    FA.reset(markF)
    gnorm_sb = BA.take(64 * 1024)
    owb = BA.take(2 * 1024)
    owf = FA.take(2 * 1024)
    xt_f = FA.take(1024)
    u1_f = FA.take(1024)
    sqt_f = FA.take(1024)
    meanb = FA.take(1024)
    rstdb = FA.take(1024)
    tmpb = FA.take(1024)

    barrier("sync")
    t_gn = 0
    for k in range(64):
        t_gn = DMA("sync", gnorm_sb[:, k * 1024:(k + 1) * 1024],
                   gn_hbm[k * 128:(k + 1) * 128, :])
    barrier("tensor")
    ow_use = [0, 0]
    u1_done = [0] * 32
    tsq_m = [0] * 32
    tst_m = [0] * 32
    tub_m = [0] * 32
    for m in range(32):
        if m >= 1:
            W("tensor", "vS", u1_done[m - 1])
        for ko in range(8):
            buf = ko % 2
            W("sync", "tS", ow_use[buf])
            td = DMA("sync", owf[:, buf * 1024:(buf + 1) * 1024],
                     owt[m, :, ko * 1024:(ko + 1) * 1024])
            W("scalar", "dS", td)
            tc = I("scalar", lambda s, buf=buf: s.activation(
                owb[:, buf * 1024:(buf + 1) * 1024],
                owf[:, buf * 1024:(buf + 1) * 1024], AF.Copy), "sS")
            W("tensor", "sS", tc)
            if m == 0 and ko == 0:
                W("tensor", "dS", t_gn)
            for kl in range(8):
                k = ko * 8 + kl
                stat = owb[:, buf * 1024 + kl * 128:buf * 1024 + (kl + 1) * 128]
                mv = gnorm_sb[:, k * 1024:(k + 1) * 1024]
                a, b = (k == 0), (k == 63)
                I("tensor", lambda t, s=stat, mv=mv, a=a, b=b: t.matmul(
                    pA[:, :], s, mv[:, 0:512], start=a, stop=b))
                tmm = I("tensor", lambda t, s=stat, mv=mv, a=a, b=b: t.matmul(
                    pB[:, :], s, mv[:, 512:1024], start=a, stop=b), "tS")
            ow_use[buf] = sems["tS"].n
        if m >= 1:
            W("sync", "vS", u1_done[m - 1])
        tdx = DMA("sync", xt_f, x_fm[m * 128:(m + 1) * 128, 3:1027])
        if m >= 1:
            W("vector", "sS", tsq_m[m - 1])
            W("vector", "tS", tst_m[m - 1])
            W("vector", "dG", tub_m[m - 1])
        W("vector", "tS", tmm)
        W("vector", "dS", tdx)
        I("vector", lambda v: v.tensor_tensor(u1_f[:, 0:512], pA[:, :], xt_f[:, 0:512],
                                              AL.add))
        u1_done[m] = I("vector", lambda v: v.tensor_tensor(
            u1_f[:, 512:1024], pB[:, :], xt_f[:, 512:1024], AL.add), "vS")
        W("scalar", "vS", u1_done[m])
        if m >= 1:
            W("scalar", "tS", tst_m[m - 1])
        tsq_m[m] = I("scalar", lambda s: s.activation(sqt_f, u1_f, AF.Square), "sS")
        W("gpsimd", "vS", u1_done[m])
        tub_m[m] = DMA("gpsimd", u1_hbm[m * 128:(m + 1) * 128, :], u1_f, s="dG")
        W("tensor", "vS", u1_done[m])
        W("tensor", "sS", tsq_m[m])
        a, b = (m == 0), (m == 31)
        I("tensor", lambda t, a=a, b=b: t.matmul(pC[:, :], onesf_sb, u1_f[:, 0:512],
                                                 start=a, stop=b))
        I("tensor", lambda t, a=a, b=b: t.matmul(pD[:, :], onesf_sb, u1_f[:, 512:1024],
                                                 start=a, stop=b))
        I("tensor", lambda t, a=a, b=b: t.matmul(pE[:, :], onesf_sb, sqt_f[:, 0:512],
                                                 start=a, stop=b))
        tst_m[m] = I("tensor", lambda t, a=a, b=b: t.matmul(
            pF[:, :], onesf_sb, sqt_f[:, 512:1024], start=a, stop=b), "tS")
    W("vector", "tS", tst_m[31])
    I("vector", lambda v: v.tensor_scalar(meanb[:, 0:512], pC[:, :], 1.0 / HID, None,
                                          AL.mult, AL.bypass))
    I("vector", lambda v: v.tensor_scalar(meanb[:, 512:1024], pD[:, :], 1.0 / HID, None,
                                          AL.mult, AL.bypass))
    I("vector", lambda v: v.tensor_tensor(tmpb, meanb, meanb, AL.mult))
    I("vector", lambda v: v.tensor_scalar(rstdb[:, 0:512], pE[:, :], 1.0 / HID, None,
                                          AL.mult, AL.bypass))
    I("vector", lambda v: v.tensor_scalar(rstdb[:, 512:1024], pF[:, :], 1.0 / HID, None,
                                          AL.mult, AL.bypass))
    tvv = I("vector", lambda v: v.tensor_tensor(rstdb, rstdb, tmpb, AL.subtract), "vS")
    W("scalar", "vS", tvv)
    I("scalar", lambda s: s.activation(rstdb, rstdb, AF.Ln, bias=small_sb[:, 5:6]))
    trs1 = I("scalar", lambda s: s.activation(rstdb, rstdb, AF.Exp, scale=-0.5), "sS")

    scope("B2")
    BA.reset(markB)
    ubf = BA.take(32 * 1024)
    u_f = FA.take(1024)
    barrier("sync")
    u_dma = 0
    ub_done = [0] * 32
    for m in range(32):
        td = DMA("sync", u1_f, u1_hbm[m * 128:(m + 1) * 128, :])
        if m == 0:
            W("vector", "sS", trs1)
        W("vector", "dS", td)
        W("vector", "dG", u_dma)
        I("vector", lambda v: v.tensor_tensor(u_f, u1_f, meanb, AL.subtract))
        tu = I("vector", lambda v: v.tensor_tensor(u_f, u_f, rstdb, AL.mult), "vS")
        ub_done[m] = I("vector", lambda v, m=m: v.tensor_copy(
            ubf[:, m * 1024:(m + 1) * 1024], u_f), "vS")
        W("gpsimd", "vS", tu)
        u_dma = DMA("gpsimd", u_hbm[m * 128:(m + 1) * 128, :], u_f, s="dG")
        W("sync", "vS", tu)

    scope("C")
    fwf = owf
    fwb = BA.take(2 * 1024)
    latb = BA.take(2 * 128)
    lbmb = BA.take(4096)
    hbf = BA.take(1024)
    u2_f = FA.take(1024)
    outt_f = FA.take(1024)

    barrier("sync")
    td = DMA("sync", xt_f, lbm[:, 0:1024])
    td2 = DMA("sync", sqt_f, lbm[:, 1024:2048])
    W("scalar", "dS", td2)
    I("scalar", lambda s: s.activation(lbmb[:, 0:1024], xt_f, AF.Copy))
    tc1 = I("scalar", lambda s: s.activation(lbmb[:, 1024:2048], sqt_f, AF.Copy), "sS")
    W("sync", "sS", tc1)
    td = DMA("sync", xt_f, lbm[:, 2048:3072])
    td2 = DMA("sync", sqt_f, lbm[:, 3072:4096])
    W("scalar", "dS", td2)
    I("scalar", lambda s: s.activation(lbmb[:, 2048:3072], xt_f, AF.Copy))
    tlbm = I("scalar", lambda s: s.activation(lbmb[:, 3072:4096], sqt_f, AF.Copy), "sS")

    barrier("tensor")
    lat_use = [0, 0]
    for k in range(32):
        buf = k % 2
        W("sync", "tS", lat_use[buf])
        td = DMA("sync", xin_sb[:, 0:128], lat[k * 128:(k + 1) * 128, :])
        W("scalar", "dS", td)
        tc = I("scalar", lambda s, buf=buf: s.activation(
            latb[:, buf * 128:(buf + 1) * 128], xin_sb[:, 0:128], AF.Copy), "sS")
        W("tensor", "sS", tc)
        W("tensor", "vS", ub_done[k])
        a, b = (k == 0), (k == 31)
        I("tensor", lambda t, k=k, a=a, b=b, buf=buf: t.matmul(
            pC[:, :], latb[:, buf * 128:(buf + 1) * 128],
            ubf[:, k * 1024:k * 1024 + 512], start=a, stop=b))
        lat_use[buf] = I("tensor", lambda t, k=k, a=a, b=b, buf=buf: t.matmul(
            pD[:, :], latb[:, buf * 128:(buf + 1) * 128],
            ubf[:, k * 1024 + 512:(k + 1) * 1024], start=a, stop=b), "tS")
    thh = lat_use[31 % 2]
    W("vector", "tS", thh)
    W("vector", "sS", t_small)
    I("vector", lambda v: v.tensor_scalar(hbf[:, 0:512], pC[:, :], small_sb[:, 4:5],
                                          None, AL.mult, AL.bypass))
    thb = I("vector", lambda v: v.tensor_scalar(hbf[:, 512:1024], pD[:, :],
                                                small_sb[:, 4:5], None, AL.mult,
                                                AL.bypass), "vS")

    fw_use = [0, 0]
    u2_done = [0] * 32
    tp_evac = 0
    u2_dma = 0
    for m in range(32):
        if m >= 1:
            W("tensor", "vS", u2_done[m - 1])
        for ko in range(4):
            buf = ko % 2
            W("sync", "tS", fw_use[buf])
            td = DMA("sync", fwf[:, buf * 1024:(buf + 1) * 1024],
                     fwt[m, :, ko * 1024:(ko + 1) * 1024])
            W("scalar", "dS", td)
            tc = I("scalar", lambda s, buf=buf: s.activation(
                fwb[:, buf * 1024:(buf + 1) * 1024],
                fwf[:, buf * 1024:(buf + 1) * 1024], AF.Copy), "sS")
            W("tensor", "sS", tc)
            for kl in range(8):
                k = ko * 8 + kl
                stat = fwb[:, buf * 1024 + kl * 128:buf * 1024 + (kl + 1) * 128]
                mv = ubf[:, k * 1024:(k + 1) * 1024]
                a = (k == 0)
                I("tensor", lambda t, s=stat, mv=mv, a=a: t.matmul(
                    pA[:, :], s, mv[:, 0:512], start=a, stop=False))
                tmm = I("tensor", lambda t, s=stat, mv=mv, a=a: t.matmul(
                    pB[:, :], s, mv[:, 512:1024], start=a, stop=False), "tS")
            fw_use[buf] = sems["tS"].n
        if m == 0:
            W("tensor", "vS", thb)
            W("tensor", "sS", tlbm)
        I("tensor", lambda t, m=m: t.matmul(pA[:, :], lbmb[:, m * 128:(m + 1) * 128],
                                            hbf[:, 0:512], start=False, stop=True))
        tmm = I("tensor", lambda t, m=m: t.matmul(pB[:, :],
                                                  lbmb[:, m * 128:(m + 1) * 128],
                                                  hbf[:, 512:1024], start=False,
                                                  stop=True), "tS")
        if m >= 1:
            W("sync", "vS", u2_done[m - 1])
        tdu = DMA("sync", u_f, u_hbm[m * 128:(m + 1) * 128, :])
        W("vector", "tS", tmm)
        W("vector", "dS", tdu)
        I("vector", lambda v, m=m: v.scalar_tensor_tensor(
            u2_f[:, 0:512], pA[:, :], fbias_sb[:, m:m + 1], u_f[:, 0:512],
            AL.add, AL.add))
        u2_done[m] = I("vector", lambda v, m=m: v.scalar_tensor_tensor(
            u2_f[:, 512:1024], pB[:, :], fbias_sb[:, m:m + 1], u_f[:, 512:1024],
            AL.add, AL.add), "vS")
        for s8 in range(8):
            W("tensor", "vS", u2_done[m])
            W("tensor", "vS", tp_evac)
            ttr = I("tensor", lambda t, s8=s8: t.transpose(
                pU[:, 0:128], u2_f[:, s8 * 128:(s8 + 1) * 128], eyef_sb), "tS")
            W("vector", "tS", ttr)
            W("vector", "dG", u2_dma)
            tp_evac = I("vector", lambda v: v.tensor_copy(outt_f[:, 0:128],
                                                          pU[:, 0:128]), "vS")
            W("gpsimd", "vS", tp_evac)
            u2_dma = DMA("gpsimd",
                         u2_hbm[s8 * 128:(s8 + 1) * 128, m * 128:(m + 1) * 128],
                         outt_f[:, 0:128], s="dG")

    scope("D")
    FA.reset(markF)
    mu_d = FA.take(4)
    row_f = FA.take(4096)
    sq_d = FA.take(4096)
    barrier("sync")
    tout = 0
    for s8 in range(8):
        td = DMA("sync", row_f, u2_hbm[s8 * 128:(s8 + 1) * 128, :])
        W("vector", "dO", tout)
        W("vector", "dS", td)
        I("vector", lambda v: v.tensor_reduce(mu_d[:, 0:1], row_f,
                                              mybir.AxisListType.X, AL.add))
        I("vector", lambda v: v.tensor_tensor(sq_d, row_f, row_f, AL.mult))
        I("vector", lambda v: v.tensor_reduce(mu_d[:, 1:2], sq_d,
                                              mybir.AxisListType.X, AL.add))
        I("vector", lambda v: v.tensor_scalar(mu_d[:, 0:1], mu_d[:, 0:1], 1.0 / HID,
                                              None, AL.mult, AL.bypass))
        I("vector", lambda v: v.tensor_scalar(mu_d[:, 1:2], mu_d[:, 1:2], 1.0 / HID,
                                              None, AL.mult, AL.bypass))
        I("vector", lambda v: v.tensor_tensor(mu_d[:, 2:3], mu_d[:, 0:1], mu_d[:, 0:1],
                                              AL.mult))
        tvar = I("vector", lambda v: v.tensor_tensor(mu_d[:, 1:2], mu_d[:, 1:2],
                                                     mu_d[:, 2:3], AL.subtract), "vS")
        W("scalar", "vS", tvar)
        I("scalar", lambda s: s.activation(mu_d[:, 1:2], mu_d[:, 1:2], AF.Ln,
                                           bias=small_sb[:, 5:6]))
        trs2 = I("scalar", lambda s: s.activation(mu_d[:, 1:2], mu_d[:, 1:2], AF.Exp,
                                                  scale=-0.5), "sS")
        W("vector", "sS", trs2)
        tno = I("vector", lambda v: v.tensor_scalar(sq_d, row_f, mu_d[:, 0:1],
                                                    mu_d[:, 1:2], AL.subtract,
                                                    AL.mult), "vS")
        W("sync", "vS", tno)
        tout = DMA("sync", y_out[s8 * 128:(s8 + 1) * 128, :], sq_d, s="dO")
        W("sync", "dO", tout)

    W("gpsimd", "dO", sems["dO"].n)

    with nc.Block() as blk:
        @blk.sync
        def _(eng):
            for f in seq["sync"]:
                f(eng)

        @blk.tensor
        def _(eng):
            for f in seq["tensor"]:
                f(eng)

        @blk.vector
        def _(eng):
            for f in seq["vector"]:
                f(eng)

        @blk.scalar
        def _(eng):
            for f in seq["scalar"]:
                f(eng)

        @blk.gpsimd
        def _(eng):
            for f in seq["gpsimd"]:
                f(eng)

    st.close()
    return nc


def _host_inputs(inp):
    import ml_dtypes

    x = np.asarray(inp["L0_attn_input_states"], np.float32)
    ipw = np.asarray(inp["L0_attn_in_proj_w"], np.float32)
    cw = np.asarray(inp["L0_attn_conv_w"], np.float32)[:, 0, :]
    cb = np.asarray(inp["L0_attn_conv_b"], np.float32)
    dtb = np.asarray(inp["L0_attn_dt_bias"], np.float32)
    al = np.asarray(inp["L0_attn_A_log"], np.float32)
    Dp = np.asarray(inp["L0_attn_D"], np.float32)
    nw = np.asarray(inp["L0_attn_norm_w"], np.float32)
    ow = np.asarray(inp["L0_attn_out_w"], np.float32)
    lA = np.asarray(inp["L0_ffn_lora_A"], np.float32)
    lB = np.asarray(inp["L0_ffn_lora_B"], np.float32)
    Wf = np.asarray(inp["L0_ffn_W"], np.float32)
    fb = np.asarray(inp["L0_ffn_bias"], np.float32)
    lw = np.asarray(inp["L0_ffn_lora_weights"], np.float32)

    INTER = 8192
    order = []
    for g in range(8):
        order += list(range(INTER + g * 1024, INTER + (g + 1) * 1024))
        order += list(range(16384 + g * 128, 16384 + (g + 1) * 128))
        order += list(range(17408 + g * 128, 17408 + (g + 1) * 128))
        order += list(range(g * 1024, (g + 1) * 1024))
    wt_perm = ipw[order].T
    wt_r = np.ascontiguousarray(
        wt_perm.reshape(32, 128, 8, 18, 128).transpose(2, 3, 1, 0, 4)
        .reshape(8, 18, 128, 4096))
    wdt_h = np.ascontiguousarray(
        ipw[18432:18560, :].reshape(128, 32, 128).transpose(2, 1, 0).reshape(128, 4096))

    conv_order = []
    for g in range(8):
        conv_order += list(range(g * 1024, (g + 1) * 1024))
        conv_order += list(range(8192 + g * 128, 8192 + (g + 1) * 128))
        conv_order += list(range(9216 + g * 128, 9216 + (g + 1) * 128))
    cwp = cw[conv_order]
    cbp = cb[conv_order]
    convw_h = np.ascontiguousarray(cwp.reshape(80, 128, 4).transpose(1, 0, 2)
                                   .reshape(128, 320))
    convb_h = np.ascontiguousarray(cbp.reshape(80, 128).T)

    ddiag_h = np.zeros((8, 128, 1024), np.float32)
    for g in range(8):
        drep = np.repeat(Dp[g * 16:(g + 1) * 16], 64)
        for pk in range(8):
            ddiag_h[g, :, pk * 128:(pk + 1) * 128] = np.diag(drep[pk * 128:(pk + 1) * 128])

    owt_h = np.ascontiguousarray(
        ow.T.reshape(64, 128, 32, 128).transpose(2, 1, 0, 3).reshape(32, 128, 8192))
    normw_h = np.ascontiguousarray(nw.reshape(64, 128).T)
    fwt_h = np.ascontiguousarray(
        Wf.T.reshape(32, 128, 32, 128).transpose(2, 1, 0, 3).reshape(32, 128, 4096))
    lat_h = np.ascontiguousarray(lA.reshape(128, 4096).T)
    lbm_h = np.ascontiguousarray(lB.transpose(0, 2, 1).reshape(128, 4096))
    lw_h = np.ascontiguousarray(np.repeat(lw, 16).reshape(128, 1))
    fbias_h = np.ascontiguousarray(fb.reshape(32, 128).T)

    ii = np.arange(128)
    addmask_h = np.where(ii[None, :] >= ii[:, None], 0.0, -1e30).astype(np.float32)
    allones_h = np.ones((128, 128), np.float32)
    eyef_h = np.eye(128, dtype=np.float32)
    eyeb_h = np.eye(128, dtype=ml_dtypes.bfloat16)
    consts_h = np.zeros((128, 2), np.float32)
    consts_h[:, 0] = 1e-5
    consts_h[:, 1] = 1.0

    shared = dict(
        wt=wt_r, wdt=wdt_h, convw=convw_h, convb=convb_h,
        dtbias=np.ascontiguousarray(dtb.reshape(128, 1)),
        alog=np.ascontiguousarray(al.reshape(128, 1)), ddiag=ddiag_h,
        owt=owt_h, normw=normw_h, fwt=fwt_h, lat=lat_h, lbm=lbm_h,
        lwcol=lw_h, fbias=fbias_h, addmask=addmask_h, allones=allones_h,
        eyef=eyef_h, eyeb=eyeb_h, consts=consts_h,
    )

    in_maps = []
    for c in range(8):
        b, half = c // 2, c % 2
        s0 = half * 1024
        if half == 0:
            halo = np.zeros((3, 4096), np.float32)
        else:
            halo = x[b, s0 - 3:s0]
        x_fm_h = np.ascontiguousarray(np.concatenate([halo, x[b, s0:s0 + 1024]], 0).T)
        m = dict(shared)
        m["x_fm"] = x_fm_h
        m["cmask"] = np.full((128, 1), float(half), np.float32)
        in_maps.append(m)
    return in_maps


def kernel(**inputs):
    in_maps = _host_inputs(inputs)
    nc = build_program()
    res = run_bass_kernel_spmd(nc, in_maps, core_ids=list(range(8)))
    out = np.empty((4, 2048, 4096), np.float32)
    for c in range(8):
        b, half = c // 2, c % 2
        out[b, half * 1024:(half + 1) * 1024, :] = res.results[c]["y_out"]
    return out
